# version 03
# baseline (speedup 1.0000x reference)
from contextlib import ExitStack

import numpy as np

import concourse.bass as bass
import concourse.mybir as mybir
from concourse.bass_utils import run_bass_kernel_spmd

F32 = mybir.dt.float32
AF = mybir.ActivationFunctionType
ALU = mybir.AluOpType

NC = 8
D = 1024
DC = 8
TB = 2048
M = 4096
TT = 512
NT = 8
TP = 256
NTP = 16
NH = 16
DM = 4096
NP = 2048
NEXP = 8
LN_EPS = 1e-5
COS_EPS = 1e-6
PI = float(np.pi)
_DEBUG_TAPS = False


class KB:
    def __init__(self, nc, es):
        self.sem = {}
        self.cnt = {}
        for s in ("ld", "sw", "cc", "t", "v", "s", "g"):
            self.sem[s] = es.enter_context(nc.semaphore(s))
            self.cnt[s] = 0

    def inc(self, ins, s, n=1):
        self.cnt[s] += n
        return ins.then_inc(self.sem[s], n)

    def w(self, eng, *names):
        for s in names:
            if self.cnt[s]:
                eng.wait_ge(self.sem[s], self.cnt[s])

    def wat(self, eng, s, v):
        if v > 0:
            eng.wait_ge(self.sem[s], v)


def build_program():
    nc = bass.Bass("TRN2", target_bir_lowering=False, debug=False)

    def dr_in(name, shape):
        return nc.dram_tensor(name, list(shape), F32, kind="ExternalInput").ap()

    x_fm = dr_in("x_fm", (D, M))
    x_tm = dr_in("x_tm", (M, D))
    wq = dr_in("wq", (D, NH))
    wk = dr_in("wk", (D, NH))
    wv = dr_in("wv", (D, D))
    wo = dr_in("wo", (D, D))
    w1 = dr_in("w1", (D, DM))
    b1 = dr_in("b1", (128, 32))
    w2 = dr_in("w2", (DM, D))
    b2 = dr_in("b2", (128, DC))
    ln1w = dr_in("ln1w", (128, DC))
    ln1b = dr_in("ln1b", (128, DC))
    ln2w = dr_in("ln2w", (128, DC))
    ln2b = dr_in("ln2b", (128, DC))
    phi = dr_in("phi", (D, NP))
    wexp = dr_in("wexp", (NEXP, D, D))
    bexp = dr_in("bexp", (1, NEXP * D))
    maskv = dr_in("maskv", (8, 1))
    eye = dr_in("eye", (128, 128))
    esel = dr_in("esel", (NH, 1024))

    out_fm = nc.dram_tensor("out_fm", [D, M], F32, kind="ExternalOutput").ap()

    def dr(name, shape):
        return nc.dram_tensor(name, list(shape), F32).ap()

    xn_fm_d = dr("xn_fm_d", (D, M))
    xn_tm_d = dr("xn_tm_d", (M, D))
    ckv_d = dr("ckv_d", (D, M))
    skv_d = dr("skv_d", (D, M))
    x1_fm_d = dr("x1_fm_d", (D, M))
    y2_fm_d = dr("y2_fm_d", (D, M))
    h_fm_d = dr("h_fm_d", (DM, M))
    s_fm_d = dr("s_fm_d", (D, M))
    disp_d = dr("disp_d", (M, NP))
    comb_d = dr("comb_d", (M, NP))
    stats_d = dr("stats_d", (2, M))
    y1_dbg = dr("y1_dbg", (D, M)) if _DEBUG_TAPS else None
    trig_dbg = dr("trig_dbg", (4 * NH, M)) if _DEBUG_TAPS else None
    heads_dbg = dr("heads_dbg", (D, M)) if _DEBUG_TAPS else None
    ag_src = dr("ag_src", (1, 4160))
    ag_dst = dr("ag_dst", (8, 4160))
    carry_d = dr("carry_d", (1, 4160))
    slots_src = dr("slots_src", (64, 2, 32, D))
    slots_dst = dr("slots_dst", (NEXP, 2, 32, D))
    y_src = dr("y_src", (NEXP, 64, D))
    y_dst = dr("y_dst", (64, 64, D))

    es = ExitStack()
    sbt = lambda name, shape: es.enter_context(nc.sbuf_tensor(name, list(shape), F32))
    pst = lambda name, shape: es.enter_context(nc.psum_tensor(name, list(shape), F32))

    ones = sbt("ones", [128, TT])
    eye_sb = sbt("eye_sb", [128, 128])
    esel_sb = sbt("esel_sb", [NH, 1024])
    mask_sb = sbt("mask_sb", [8, 1])
    cst = sbt("cst", [128, 4])
    ln1_sb = sbt("ln1_sb", [128, 2 * DC])
    ln2_sb = sbt("ln2_sb", [128, 2 * DC])
    b2_sb = sbt("b2_sb", [128, DC])
    b1_sb = sbt("b1_sb", [128, 32])

    pb = [pst(f"pb{i}", [128, 512]) for i in range(8)]

    with nc.Block() as block:
        kb = KB(nc, es)
        cnt = kb.cnt

        def barrier():
            def mk():
                def _(e):
                    kb.w(e, "ld", "sw", "cc", "t", "v", "s")
                return _
            block.sync(mk())
            block.tensor(mk())
            block.vector(mk())
            block.scalar(mk())
            block.gpsimd(mk())

        def _(sync):
            for dst, src in [
                (eye_sb[:, :], eye), (esel_sb[:, :], esel), (mask_sb[:, :], maskv),
                (ln1_sb[:, 0:DC], ln1w), (ln1_sb[:, DC:2 * DC], ln1b),
                (ln2_sb[:, 0:DC], ln2w), (ln2_sb[:, DC:2 * DC], ln2b),
                (b2_sb[:, :], b2), (b1_sb[:, :], b1),
            ]:
                kb.inc(sync.dma_start(out=dst, in_=src), "ld", 16)
        block.sync(_)

        def _(v):
            v.memset(ones[:, :], 1.0)
            v.memset(cst[:, 0:1], PI / 4)
            kb.inc(v.memset(cst[:, 1:2], PI / 2), "v")
        block.vector(_)

        barrier()

        es12 = ExitStack()
        sb12 = lambda name, shape: es12.enter_context(nc.sbuf_tensor(name, list(shape), F32))
        cq_sb = sb12("cq_sb", [NH, M])
        sq_sb = sb12("sq_sb", [NH, M])
        ckc_sb = sb12("ckc_sb", [NH, M])
        skc_sb = sb12("skc_sb", [NH, M])
        ckv_st = [sb12(f"ckv_st{b}", [128, DC]) for b in range(2)]
        skv_st = [sb12(f"skv_st{b}", [128, DC]) for b in range(2)]
        kst = [sb12(f"kst{b}", [NH, 2]) for b in range(2)]
        carC = [sb12(f"carC{b}", [128, DC]) for b in range(2)]
        carS = [sb12(f"carS{b}", [128, DC]) for b in range(2)]
        karr = [sb12(f"karr{b}", [NH, 2]) for b in range(2)]

        with (
            nc.sbuf_tensor("xt", [128, DC, TP], F32) as xt,
            nc.sbuf_tensor("sqb", [128, 4, TP], F32) as sqb,
            nc.sbuf_tensor("rows", [1, 12, TP], F32) as rows,
            nc.sbuf_tensor("xnt", [128, DC, TP], F32) as xnt,
            nc.sbuf_tensor("y1t", [128, DC, TP], F32) as y1t,
            nc.sbuf_tensor("wqk_sb", [128, DC, 2 * NH], F32) as wqk_sb,
            nc.sbuf_tensor("wv_sb", [128, DC, D], F32) as wv_sb,
            nc.sbuf_tensor("th_sb", [NH, 2, TP], F32) as th_sb,
            nc.sbuf_tensor("ck_t", [NH, TP], F32) as ck_t,
            nc.sbuf_tensor("ktmp", [NH, 2], F32) as ktmp,
            nc.sbuf_tensor("vtmp2", [128, 2], F32) as vtmp2,
            nc.sbuf_tensor("sk_t", [NH, TP], F32) as sk_t,
            nc.sbuf_tensor("cv_sb", [128, TP], F32) as cv_sb,
            nc.sbuf_tensor("sv_sb", [128, TP], F32) as sv_sb,
            nc.sbuf_tensor("vtmp", [128, TP], F32) as vtmp,
            nc.sbuf_tensor("ckvt", [128, TP], F32) as ckvt,
            nc.sbuf_tensor("skvt", [128, TP], F32) as skvt,
        ):
            def _(sync):
                kb.inc(sync.dma_start(out=wqk_sb[:, :, 0:NH],
                                      in_=wq.rearrange("(c p) m -> p c m", p=128)), "ld", 16)
                kb.inc(sync.dma_start(out=wqk_sb[:, :, NH:2 * NH],
                                      in_=wk.rearrange("(c p) m -> p c m", p=128)), "ld", 16)
                kb.inc(sync.dma_start(out=wv_sb[:, :, :],
                                      in_=wv.rearrange("(c p) m -> p c m", p=128)), "ld", 16)
            block.sync(_)

            for j in range(NTP):
                b = j // 8
                first = (j % 8 == 0)
                c0 = j * TP

                def _(sync, c0=c0):
                    kb.w(sync, "t", "v", "s")
                    for k in range(DC):
                        kb.inc(sync.dma_start(
                            out=xt[:, k, :],
                            in_=x_fm[k * 128:(k + 1) * 128, c0:c0 + TP]), "ld", 16)
                block.sync(_)

                s_base = cnt["s"]
                t_base = cnt["t"]
                def _(s, s_base=s_base, t_base=t_base):
                    kb.w(s, "ld")
                    for k in range(DC):
                        if k >= 4:
                            kb.wat(s, "t", t_base + (k - 4) + 1)
                        kb.inc(s.activation(sqb[:, k % 4, :], xt[:, k, :], AF.Square,
                                            bias=0.0, scale=1.0), "s")
                block.scalar(_)
                def _(t, s_base=s_base):
                    kb.w(t, "ld")
                    for k in range(DC):
                        t.matmul(pb[0][0:1, 0:TP], ones[:, 0:1], xt[:, k, :],
                                 start=(k == 0), stop=(k == DC - 1))
                        kb.wat(t, "s", s_base + k + 1)
                        kb.inc(t.matmul(pb[1][0:1, 0:TP], ones[:, 0:1], sqb[:, k % 4, :],
                                        start=(k == 0), stop=(k == DC - 1)), "t")
                block.tensor(_)

                def _(v):
                    kb.w(v, "t", "sw")
                    v.tensor_scalar(rows[:, 0, :], pb[0][0:1, 0:TP], 1.0 / D, None, op0=ALU.mult)
                    v.tensor_scalar(rows[:, 1, :], pb[1][0:1, 0:TP], 1.0 / D, None, op0=ALU.mult)
                    v.tensor_tensor(rows[:, 2, :], rows[:, 0, :], rows[:, 0, :], op=ALU.mult)
                    v.tensor_sub(rows[:, 2, :], rows[:, 1, :], rows[:, 2, :])
                    kb.inc(v.tensor_scalar(rows[:, 3, :], rows[:, 2, :], LN_EPS, None,
                                           op0=ALU.add), "v")
                block.vector(_)
                def _(s):
                    kb.w(s, "v")
                    kb.inc(s.activation(rows[:, 4, :], rows[:, 3, :], AF.Sqrt,
                                        bias=0.0, scale=1.0), "s")
                block.scalar(_)
                def _(v):
                    kb.w(v, "s")
                    v.reciprocal(rows[:, 5, :], rows[:, 4, :])
                    v.reciprocal(rows[:, 6, :], rows[:, 3, :])
                    v.tensor_tensor(rows[:, 7, :], rows[:, 2, :], rows[:, 6, :], op=ALU.mult)
                    kb.inc(v.tensor_scalar(rows[:, 8, :], rows[:, 7, :], LN_EPS, None,
                                           op0=ALU.add), "v")
                block.vector(_)
                def _(s):
                    kb.w(s, "v")
                    kb.inc(s.activation(rows[:, 9, :], rows[:, 8, :], AF.Sqrt,
                                        bias=0.0, scale=1.0), "s")
                block.scalar(_)
                def _(v):
                    kb.w(v, "s")
                    v.reciprocal(rows[:, 10, :], rows[:, 9, :])
                    kb.inc(v.tensor_tensor(rows[:, 11, :], rows[:, 5, :], rows[:, 10, :],
                                           op=ALU.mult), "v")
                block.vector(_)
                def _(g, c0=c0):
                    kb.w(g, "v")
                    kb.inc(g.dma_start(out=stats_d[0:1, c0:c0 + TP], in_=rows[:, 0, :]), "sw", 16)
                    kb.inc(g.dma_start(out=stats_d[1:2, c0:c0 + TP], in_=rows[:, 5, :]), "sw", 16)
                block.gpsimd(_)

                def _(t):
                    kb.w(t, "v")
                    t.matmul(pb[2][:, 0:TP], ones[0:1, 0:128], rows[:, 0, :], start=True, stop=True)
                    t.matmul(pb[3][:, 0:TP], ones[0:1, 0:128], rows[:, 5, :], start=True, stop=True)
                    kb.inc(t.matmul(pb[4][:, 0:TP], ones[0:1, 0:128], rows[:, 11, :],
                                    start=True, stop=True), "t")
                block.tensor(_)

                def _(v):
                    kb.w(v, "t", "sw")
                    for k in range(DC):
                        v.tensor_sub(xnt[:, k, :], xt[:, k, :], pb[2][:, 0:TP])
                        v.tensor_tensor(y1t[:, k, :], xnt[:, k, :], pb[4][:, 0:TP], op=ALU.mult)
                        v.tensor_scalar(y1t[:, k, :], y1t[:, k, :],
                                        ln1_sb[:, k:k + 1], ln1_sb[:, DC + k:DC + k + 1],
                                        op0=ALU.mult, op1=ALU.add)
                        v.tensor_tensor(xnt[:, k, :], xnt[:, k, :], pb[3][:, 0:TP], op=ALU.mult)
                    kb.inc(v.engine_nop(), "v")
                block.vector(_)
                def _(g, c0=c0):
                    kb.w(g, "v")
                    for k in range(DC):
                        kb.inc(g.dma_start(
                            out=xn_fm_d[k * 128:(k + 1) * 128, c0:c0 + TP],
                            in_=xnt[:, k, :]), "sw", 16)
                        if _DEBUG_TAPS:
                            kb.inc(g.dma_start(
                                out=y1_dbg[k * 128:(k + 1) * 128, c0:c0 + TP],
                                in_=y1t[:, k, :]), "sw", 16)
                block.gpsimd(_)

                def _(t):
                    kb.w(t, "v")
                    for k in range(DC):
                        t.matmul(pb[5][0:NH, 0:TP], wqk_sb[:, k, 0:NH], y1t[:, k, :],
                                 start=(k == 0), stop=(k == DC - 1))
                    for k in range(DC):
                        ins = t.matmul(pb[6][0:NH, 0:TP], wqk_sb[:, k, NH:2 * NH], y1t[:, k, :],
                                       start=(k == 0), stop=(k == DC - 1))
                    kb.inc(ins, "t")
                block.tensor(_)
                def _(s, c0=c0):
                    kb.w(s, "t")
                    s.activation(th_sb[:, 0, :], pb[5][0:NH, 0:TP], AF.Tanh, bias=0.0, scale=1.0)
                    s.activation(th_sb[:, 1, :], pb[6][0:NH, 0:TP], AF.Tanh, bias=0.0, scale=1.0)
                    s.activation(cq_sb[:, c0:c0 + TP], th_sb[:, 0, :], AF.Sin,
                                 bias=cst[0:NH, 1:2], scale=cst[0:NH, 0:1])
                    s.activation(sq_sb[:, c0:c0 + TP], th_sb[:, 0, :], AF.Sin,
                                 bias=0.0, scale=cst[0:NH, 0:1])
                    s.activation(ck_t[:, :], th_sb[:, 1, :], AF.Sin,
                                 bias=cst[0:NH, 1:2], scale=cst[0:NH, 0:1])
                    kb.inc(s.activation(sk_t[:, :], th_sb[:, 1, :], AF.Sin,
                                        bias=0.0, scale=cst[0:NH, 0:1]), "s")
                block.scalar(_)
                def _(v, b=b, first=first, c0=c0):
                    kb.w(v, "s")
                    if first:
                        v.memset(kst[b][:, :], 0.0)
                    init_c = 0.0 if first else kst[b][:, 0:1]
                    init_s = 0.0 if first else kst[b][:, 1:2]
                    v.reduce_sum(ktmp[:, 0:1], ck_t[:, :], axis=mybir.AxisListType.X)
                    v.reduce_sum(ktmp[:, 1:2], sk_t[:, :], axis=mybir.AxisListType.X)
                    v.tensor_tensor_scan(ckc_sb[:, c0:c0 + TP], ones[0:NH, 0:TP],
                                         ck_t[:, :], init_c,
                                         op0=ALU.mult, op1=ALU.add)
                    v.tensor_tensor_scan(skc_sb[:, c0:c0 + TP], ones[0:NH, 0:TP],
                                         sk_t[:, :], init_s,
                                         op0=ALU.mult, op1=ALU.add)
                    kb.inc(v.tensor_add(kst[b][:, :], kst[b][:, :], ktmp[:, :]), "v")
                block.vector(_)
                if _DEBUG_TAPS:
                    def _(g, c0=c0):
                        kb.w(g, "v", "s")
                        kb.inc(g.dma_start(out=trig_dbg[0:NH, c0:c0 + TP],
                                           in_=cq_sb[:, c0:c0 + TP]), "sw", 16)
                        kb.inc(g.dma_start(out=trig_dbg[NH:2 * NH, c0:c0 + TP],
                                           in_=sq_sb[:, c0:c0 + TP]), "sw", 16)
                        kb.inc(g.dma_start(out=trig_dbg[2 * NH:3 * NH, c0:c0 + TP],
                                           in_=ckc_sb[:, c0:c0 + TP]), "sw", 16)
                        kb.inc(g.dma_start(out=trig_dbg[3 * NH:4 * NH, c0:c0 + TP],
                                           in_=skc_sb[:, c0:c0 + TP]), "sw", 16)
                    block.gpsimd(_)

                for mch in range(DC):
                    vb_, cb_, sb_ = pb[0 + (mch % 2)], pb[2 + (mch % 2)], pb[4 + (mch % 2)]
                    def _(t, mch=mch, vb_=vb_, cb_=cb_, sb_=sb_):
                        kb.w(t, "v")
                        for k in range(DC):
                            t.matmul(vb_[:, 0:TP], wv_sb[:, k, mch * 128:(mch + 1) * 128],
                                     y1t[:, k, :], start=(k == 0), stop=(k == DC - 1))
                        t.matmul(cb_[:, 0:TP], esel_sb[:, mch * 128:(mch + 1) * 128],
                                 ck_t[:, :], start=True, stop=True)
                        kb.inc(t.matmul(sb_[:, 0:TP], esel_sb[:, mch * 128:(mch + 1) * 128],
                                        sk_t[:, :], start=True, stop=True), "t")
                    block.tensor(_)
                    def _(v, mch=mch, vb_=vb_, cb_=cb_, sb_=sb_, b=b, first=first):
                        kb.w(v, "t", "sw")
                        v.tensor_copy(vtmp[:, :], vb_[:, 0:TP])
                        v.tensor_tensor(cv_sb[:, :], vtmp[:, :], cb_[:, 0:TP], op=ALU.mult)
                        v.tensor_tensor(sv_sb[:, :], vtmp[:, :], sb_[:, 0:TP], op=ALU.mult)
                        if first and mch == 0:
                            v.memset(ckv_st[b][:, :], 0.0)
                            v.memset(skv_st[b][:, :], 0.0)
                        ic = 0.0 if first else ckv_st[b][:, mch:mch + 1]
                        isn = 0.0 if first else skv_st[b][:, mch:mch + 1]
                        v.reduce_sum(vtmp2[:, 0:1], cv_sb[:, :], axis=mybir.AxisListType.X)
                        v.reduce_sum(vtmp2[:, 1:2], sv_sb[:, :], axis=mybir.AxisListType.X)
                        v.tensor_tensor_scan(ckvt[:, :], ones[:, 0:TP], cv_sb[:, :], ic,
                                             op0=ALU.mult, op1=ALU.add)
                        v.tensor_tensor_scan(skvt[:, :], ones[:, 0:TP], sv_sb[:, :], isn,
                                             op0=ALU.mult, op1=ALU.add)
                        v.tensor_add(ckv_st[b][:, mch:mch + 1], ckv_st[b][:, mch:mch + 1],
                                     vtmp2[:, 0:1])
                        kb.inc(v.tensor_add(skv_st[b][:, mch:mch + 1],
                                            skv_st[b][:, mch:mch + 1], vtmp2[:, 1:2]), "v")
                    block.vector(_)
                    def _(g, mch=mch, c0=c0):
                        kb.w(g, "v")
                        kb.inc(g.dma_start(
                            out=ckv_d[mch * 128:(mch + 1) * 128, c0:c0 + TP],
                            in_=ckvt[:, :]), "sw", 16)
                        kb.inc(g.dma_start(
                            out=skv_d[mch * 128:(mch + 1) * 128, c0:c0 + TP],
                            in_=skvt[:, :]), "sw", 16)
                    block.gpsimd(_)

            def _(g):
                kb.w(g, "v", "s", "t")
                for b in range(2):
                    off = b * 2080
                    kb.inc(g.dma_start(out=ag_src[0:1, off:off + 1024],
                                       in_=ckv_st[b][:, :]), "sw", 16)
                    kb.inc(g.dma_start(out=ag_src[0:1, off + 1024:off + 2048],
                                       in_=skv_st[b][:, :]), "sw", 16)
                    kb.inc(g.dma_start(out=ag_src[0:1, off + 2048:off + 2080],
                                       in_=kst[b][:, :]), "sw", 16)
                kb.w(g, "sw")
                kb.inc(g.collective_compute(
                    "AllGather", ALU.bypass, replica_groups=[list(range(NC))],
                    ins=[ag_src.opt()], outs=[ag_dst.opt()]), "cc")
            block.gpsimd(_)

        with (
            nc.sbuf_tensor("g_sb", [8, 4160], F32) as g_sb,
            nc.sbuf_tensor("crow", [1, 4160], F32) as crow,
            nc.sbuf_tensor("stt_sb", [128, 2, 32], F32) as stt_sb,
            nc.sbuf_tensor("xtm", [128, 2, D], F32) as xtm,
            nc.sbuf_tensor("xntm", [128, 2, D], F32) as xntm,
        ):
            def _(g):
                kb.w(g, "cc")
                kb.inc(g.dma_start(out=g_sb[:, :], in_=ag_dst), "sw", 16)
            block.gpsimd(_)
            t_base = cnt["t"]
            v_base = cnt["v"]
            offs = list(range(0, 4160, 512))
            def _(t, t_base=t_base, v_base=v_base):
                kb.w(t, "sw")
                for i, off in enumerate(offs):
                    wdt = min(512, 4160 - off)
                    if i >= 1:
                        kb.wat(t, "v", v_base + i)
                    kb.inc(t.matmul(pb[7][0:1, 0:wdt], mask_sb[:, :],
                                    g_sb[:, off:off + wdt], start=True, stop=True), "t")
            block.tensor(_)
            def _(v, t_base=t_base):
                for i, off in enumerate(offs):
                    wdt = min(512, 4160 - off)
                    kb.wat(v, "t", t_base + i + 1)
                    kb.inc(v.tensor_copy(crow[0:1, off:off + wdt], pb[7][0:1, 0:wdt]), "v")
            block.vector(_)
            def _(g):
                kb.w(g, "v")
                kb.inc(g.dma_start(out=carry_d, in_=crow[:, :]), "sw", 16)
            block.gpsimd(_)
            def _(sync):
                kb.w(sync, "sw")
                for b in range(2):
                    off = b * 2080
                    kb.inc(sync.dma_start(
                        out=carC[b][:, :],
                        in_=carry_d[0:1, off:off + 1024].rearrange("o (p k) -> (o p) k", p=128)),
                        "ld", 16)
                    kb.inc(sync.dma_start(
                        out=carS[b][:, :],
                        in_=carry_d[0:1, off + 1024:off + 2048].rearrange("o (p k) -> (o p) k", p=128)),
                        "ld", 16)
                    kb.inc(sync.dma_start(
                        out=karr[b][:, :],
                        in_=carry_d[0:1, off + 2048:off + 2080].rearrange("o (p k) -> (o p) k", p=NH)),
                        "ld", 16)
            block.sync(_)
            with nc.sbuf_tensor("strow", [32, 2, 128], F32) as strow:
                def _(sync):
                    kb.w(sync, "sw")
                    for r in range(2):
                        kb.inc(sync.dma_start(
                            out=strow[:, r, :],
                            in_=stats_d[r:r + 1, :].rearrange("o (a p) -> (o a) p", a=32)),
                            "ld", 16)
                block.sync(_)
                def _(t):
                    kb.w(t, "ld")
                    for r in range(2):
                        kb.inc(t.transpose(pb[r][0:128, 0:32], strow[:, r, :],
                                           eye_sb[0:32, 0:32]), "t")
                block.tensor(_)
                def _(v):
                    kb.w(v, "t")
                    for r in range(2):
                        kb.inc(v.tensor_copy(stt_sb[:, r, :], pb[r][0:128, 0:32]), "v")
                block.vector(_)

            for u in range(0, 32, 2):
                def _(sync, u=u):
                    if u >= 2:
                        kb.w(sync, "v")
                    for q in range(2):
                        kb.inc(sync.dma_start(out=xtm[:, q, :],
                                              in_=x_tm[(u + q) * 128:(u + q + 1) * 128, :]),
                               "ld", 16)
                block.sync(_)
                def _(v, u=u):
                    kb.w(v, "ld", "sw")
                    for q in range(2):
                        kb.inc(v.tensor_scalar(
                            xntm[:, q, :], xtm[:, q, :],
                            stt_sb[:, 0, u + q:u + q + 1], stt_sb[:, 1, u + q:u + q + 1],
                            op0=ALU.subtract, op1=ALU.mult), "v")
                block.vector(_)
                def _(g, u=u):
                    kb.w(g, "v")
                    for q in range(2):
                        kb.inc(g.dma_start(out=xn_tm_d[(u + q) * 128:(u + q + 1) * 128, :],
                                           in_=xntm[:, q, :]), "sw", 16)
                block.gpsimd(_)

        barrier()

        with (
            nc.sbuf_tensor("wo_sb", [128, DC, D], F32) as wo_sb,
            nc.sbuf_tensor("ckv_t", [128, DC, TP], F32) as ckv_t,
            nc.sbuf_tensor("skv_t", [128, DC, TP], F32) as skv_t,
            nc.sbuf_tensor("xn_t", [128, DC, TP], F32) as xn_t,
            nc.sbuf_tensor("heads", [128, DC, TP], F32) as heads,
            nc.sbuf_tensor("rden", [NH, TP], F32) as rden,
            nc.sbuf_tensor("dtmp", [NH, 2, TP], F32) as dtmp,
            nc.sbuf_tensor("htmp", [128, TP], F32) as htmp,
            nc.sbuf_tensor("x1t", [128, DC, TP], F32) as x1t,
            nc.sbuf_tensor("y2t", [128, DC, TP], F32) as y2t,
            nc.sbuf_tensor("sqb2", [128, 4, TP], F32) as sqb2,
            nc.sbuf_tensor("rows2", [1, 6, TP], F32) as rows2,
        ):
            def _(sync):
                kb.inc(sync.dma_start(out=wo_sb[:, :, :],
                                      in_=wo.rearrange("(c p) m -> p c m", p=128)), "ld", 16)
            block.sync(_)

            for j in range(NTP):
                b = j // 8
                c0 = j * TP

                def _(sync, c0=c0):
                    kb.w(sync, "t", "v", "s")
                    for k in range(DC):
                        kb.inc(sync.dma_start(out=ckv_t[:, k, :],
                                              in_=ckv_d[k * 128:(k + 1) * 128, c0:c0 + TP]), "ld", 16)
                        kb.inc(sync.dma_start(out=skv_t[:, k, :],
                                              in_=skv_d[k * 128:(k + 1) * 128, c0:c0 + TP]), "ld", 16)
                        kb.inc(sync.dma_start(out=xn_t[:, k, :],
                                              in_=xn_fm_d[k * 128:(k + 1) * 128, c0:c0 + TP]), "ld", 16)
                block.sync(_)

                def _(v, b=b, c0=c0):
                    kb.w(v, "ld")
                    v.tensor_scalar(dtmp[:, 0, :], ckc_sb[:, c0:c0 + TP],
                                    karr[b][:, 0:1], None, op0=ALU.add)
                    v.tensor_tensor(dtmp[:, 0, :], dtmp[:, 0, :], cq_sb[:, c0:c0 + TP], op=ALU.mult)
                    v.tensor_scalar(dtmp[:, 1, :], skc_sb[:, c0:c0 + TP],
                                    karr[b][:, 1:2], None, op0=ALU.add)
                    v.tensor_tensor(dtmp[:, 1, :], dtmp[:, 1, :], sq_sb[:, c0:c0 + TP], op=ALU.mult)
                    v.tensor_add(dtmp[:, 0, :], dtmp[:, 0, :], dtmp[:, 1, :])
                    v.tensor_scalar(dtmp[:, 0, :], dtmp[:, 0, :], COS_EPS, None, op0=ALU.add)
                    kb.inc(v.reciprocal(rden[:, :], dtmp[:, 0, :]), "v")
                block.vector(_)

                for mch in range(DC):
                    cb_ = pb[0 + (mch % 2) * 3]
                    sb_ = pb[1 + (mch % 2) * 3]
                    rb_ = pb[2 + (mch % 2) * 3]
                    def _(t, mch=mch, cb_=cb_, sb_=sb_, rb_=rb_, c0=c0):
                        kb.w(t, "v")
                        t.matmul(cb_[:, 0:TP], esel_sb[:, mch * 128:(mch + 1) * 128],
                                 cq_sb[:, c0:c0 + TP], start=True, stop=True)
                        t.matmul(sb_[:, 0:TP], esel_sb[:, mch * 128:(mch + 1) * 128],
                                 sq_sb[:, c0:c0 + TP], start=True, stop=True)
                        kb.inc(t.matmul(rb_[:, 0:TP], esel_sb[:, mch * 128:(mch + 1) * 128],
                                        rden[:, :], start=True, stop=True), "t")
                    block.tensor(_)
                    def _(v, mch=mch, cb_=cb_, sb_=sb_, rb_=rb_, b=b):
                        kb.w(v, "t")
                        v.scalar_tensor_tensor(heads[:, mch, :], ckv_t[:, mch, :],
                                               carC[b][:, mch:mch + 1], cb_[:, 0:TP],
                                               op0=ALU.add, op1=ALU.mult)
                        v.scalar_tensor_tensor(htmp[:, :], skv_t[:, mch, :],
                                               carS[b][:, mch:mch + 1], sb_[:, 0:TP],
                                               op0=ALU.add, op1=ALU.mult)
                        v.tensor_add(heads[:, mch, :], heads[:, mch, :], htmp[:, :])
                        kb.inc(v.tensor_tensor(heads[:, mch, :], heads[:, mch, :],
                                               rb_[:, 0:TP], op=ALU.mult), "v")
                    block.vector(_)

                if _DEBUG_TAPS:
                    def _(g, c0=c0):
                        kb.w(g, "v")
                        for k in range(DC):
                            kb.inc(g.dma_start(
                                out=heads_dbg[k * 128:(k + 1) * 128, c0:c0 + TP],
                                in_=heads[:, k, :]), "sw", 16)
                    block.gpsimd(_)
                for dch in range(DC):
                    ob = pb[6 + (dch % 2)]
                    def _(t, dch=dch, ob=ob):
                        kb.w(t, "v")
                        if _DEBUG_TAPS:
                            kb.w(t, "sw")
                        for mch in range(DC):
                            ins = t.matmul(ob[:, 0:TP], wo_sb[:, mch, dch * 128:(dch + 1) * 128],
                                           heads[:, mch, :],
                                           start=(mch == 0), stop=(mch == DC - 1))
                        kb.inc(ins, "t")
                    block.tensor(_)
                    def _(v, dch=dch, ob=ob):
                        kb.w(v, "t", "sw")
                        kb.inc(v.tensor_add(x1t[:, dch, :], xn_t[:, dch, :], ob[:, 0:TP]), "v")
                    block.vector(_)

                s_base = cnt["s"]
                t_base = cnt["t"]
                def _(s, s_base=s_base, t_base=t_base):
                    kb.w(s, "v")
                    for k in range(DC):
                        if k >= 4:
                            kb.wat(s, "t", t_base + (k - 4) + 1)
                        kb.inc(s.activation(sqb2[:, k % 4, :], x1t[:, k, :], AF.Square,
                                            bias=0.0, scale=1.0), "s")
                block.scalar(_)
                def _(t, s_base=s_base):
                    kb.w(t, "v")
                    for k in range(DC):
                        t.matmul(pb[0][0:1, 0:TP], ones[:, 0:1], x1t[:, k, :],
                                 start=(k == 0), stop=(k == DC - 1))
                        kb.wat(t, "s", s_base + k + 1)
                        kb.inc(t.matmul(pb[1][0:1, 0:TP], ones[:, 0:1], sqb2[:, k % 4, :],
                                        start=(k == 0), stop=(k == DC - 1)), "t")
                block.tensor(_)
                def _(v):
                    kb.w(v, "t")
                    v.tensor_scalar(rows2[:, 0, :], pb[0][0:1, 0:TP], 1.0 / D, None, op0=ALU.mult)
                    v.tensor_scalar(rows2[:, 1, :], pb[1][0:1, 0:TP], 1.0 / D, None, op0=ALU.mult)
                    v.tensor_tensor(rows2[:, 2, :], rows2[:, 0, :], rows2[:, 0, :], op=ALU.mult)
                    v.tensor_sub(rows2[:, 2, :], rows2[:, 1, :], rows2[:, 2, :])
                    kb.inc(v.tensor_scalar(rows2[:, 3, :], rows2[:, 2, :], LN_EPS, None,
                                           op0=ALU.add), "v")
                block.vector(_)
                def _(s):
                    kb.w(s, "v")
                    kb.inc(s.activation(rows2[:, 4, :], rows2[:, 3, :], AF.Sqrt,
                                        bias=0.0, scale=1.0), "s")
                block.scalar(_)
                def _(v):
                    kb.w(v, "s")
                    kb.inc(v.reciprocal(rows2[:, 5, :], rows2[:, 4, :]), "v")
                block.vector(_)
                def _(t):
                    kb.w(t, "v")
                    t.matmul(pb[2][:, 0:TP], ones[0:1, 0:128], rows2[:, 0, :], start=True, stop=True)
                    kb.inc(t.matmul(pb[3][:, 0:TP], ones[0:1, 0:128], rows2[:, 5, :],
                                    start=True, stop=True), "t")
                block.tensor(_)
                def _(v):
                    kb.w(v, "t", "sw")
                    for k in range(DC):
                        v.tensor_sub(y2t[:, k, :], x1t[:, k, :], pb[2][:, 0:TP])
                        v.tensor_tensor(y2t[:, k, :], y2t[:, k, :], pb[3][:, 0:TP], op=ALU.mult)
                        v.tensor_scalar(y2t[:, k, :], y2t[:, k, :],
                                        ln2_sb[:, k:k + 1], ln2_sb[:, DC + k:DC + k + 1],
                                        op0=ALU.mult, op1=ALU.add)
                    kb.inc(v.engine_nop(), "v")
                block.vector(_)
                def _(g, c0=c0):
                    kb.w(g, "v")
                    for k in range(DC):
                        kb.inc(g.dma_start(out=x1_fm_d[k * 128:(k + 1) * 128, c0:c0 + TP],
                                           in_=x1t[:, k, :]), "sw", 16)
                        kb.inc(g.dma_start(out=y2_fm_d[k * 128:(k + 1) * 128, c0:c0 + TP],
                                           in_=y2t[:, k, :]), "sw", 16)
                block.gpsimd(_)

        barrier()
        es12.close()

        with (
            nc.sbuf_tensor("phi_sb", [128, DC, NP], F32) as phi_sb,
            nc.sbuf_tensor("xna", [128, DC, TT], F32) as xna,
            nc.sbuf_tensor("xnb", [128, DC, TT], F32) as xnb,
            nc.sbuf_tensor("e0", [128, NP], F32) as e0,
            nc.sbuf_tensor("e1", [128, NP], F32) as e1,
            nc.sbuf_tensor("l1s", [128, NP], F32) as l1s,
            nc.sbuf_tensor("dd", [128, NP], F32) as dd,
            nc.sbuf_tensor("zz", [128, 10], F32) as zz,
        ):
            def _(sync):
                kb.inc(sync.dma_start(out=phi_sb[:, :, :],
                                      in_=phi.rearrange("(c p) m -> p c m", p=128)), "ld", 16)
            block.sync(_)

            for jp in range(4):
                c0a, c0b = jp * TT, (jp + 4) * TT
                def _(sync, c0a=c0a, c0b=c0b):
                    kb.w(sync, "t")
                    for k in range(DC):
                        kb.inc(sync.dma_start(out=xna[:, k, :],
                                              in_=xn_fm_d[k * 128:(k + 1) * 128, c0a:c0a + TT]), "ld", 16)
                        kb.inc(sync.dma_start(out=xnb[:, k, :],
                                              in_=xn_fm_d[k * 128:(k + 1) * 128, c0b:c0b + TT]), "ld", 16)
                block.sync(_)

                for sub in range(4):
                    r0a = c0a + sub * 128
                    r0b = c0b + sub * 128
                    def _(t, sub=sub):
                        kb.w(t, "ld", "s", "v")
                        for half, xsrc in ((0, xna), (1, xnb)):
                            for npc in range(4):
                                for k in range(DC):
                                    ins = t.matmul(
                                        pb[half * 4 + npc][:, :],
                                        xsrc[:, k, sub * 128:(sub + 1) * 128],
                                        phi_sb[:, k, npc * 512:(npc + 1) * 512],
                                        start=(k == 0), stop=(k == DC - 1))
                        kb.inc(ins, "t")
                    block.tensor(_)
                    def _(s):
                        kb.w(s, "t", "sw")
                        for npc in range(4):
                            s.activation(e0[:, npc * 512:(npc + 1) * 512], pb[npc][:, :],
                                         AF.Exp, bias=0.0, scale=1.0)
                        for npc in range(4):
                            s.activation(e1[:, npc * 512:(npc + 1) * 512], pb[4 + npc][:, :],
                                         AF.Exp, bias=0.0, scale=1.0)
                        for npc in range(4):
                            ins = s.activation(l1s[:, npc * 512:(npc + 1) * 512], pb[4 + npc][:, :],
                                               AF.Identity, bias=0.0, scale=1.0)
                        kb.inc(ins, "s")
                    block.scalar(_)
                    def _(v):
                        kb.w(v, "s", "sw")
                        for npc in range(4):
                            v.tensor_sub(dd[:, npc * 512:(npc + 1) * 512],
                                         pb[npc][:, :], l1s[:, npc * 512:(npc + 1) * 512])
                        v.reduce_sum(zz[:, 0:1], e0[:, :], axis=mybir.AxisListType.X)
                        v.reduce_sum(zz[:, 4:5], e1[:, :], axis=mybir.AxisListType.X)
                        v.reciprocal(zz[:, 8:9], zz[:, 0:1])
                        kb.inc(v.reciprocal(zz[:, 9:10], zz[:, 4:5]), "v")
                    block.vector(_)
                    def _(s):
                        kb.w(s, "v")
                        kb.inc(s.activation(dd[:, :], dd[:, :], AF.Sigmoid,
                                            bias=0.0, scale=1.0), "s")
                    block.scalar(_)
                    def _(v):
                        kb.w(v, "s")
                        v.tensor_scalar(e0[:, :], e0[:, :], zz[:, 8:9], None, op0=ALU.mult)
                        v.tensor_scalar(e1[:, :], e1[:, :], zz[:, 9:10], None, op0=ALU.mult)
                        kb.inc(v.tensor_scalar(l1s[:, :], dd[:, :], -1.0, 1.0,
                                               op0=ALU.mult, op1=ALU.add), "v")
                    block.vector(_)
                    def _(g, r0a=r0a, r0b=r0b):
                        kb.w(g, "v")
                        kb.inc(g.dma_start(out=comb_d[r0a:r0a + 128, :], in_=e0[:, :]), "sw", 16)
                        kb.inc(g.dma_start(out=comb_d[r0b:r0b + 128, :], in_=e1[:, :]), "sw", 16)
                        kb.inc(g.dma_start(out=disp_d[r0a:r0a + 128, :], in_=dd[:, :]), "sw", 16)
                        kb.inc(g.dma_start(out=disp_d[r0b:r0b + 128, :], in_=l1s[:, :]), "sw", 16)
                    block.gpsimd(_)

        barrier()

        with (
            nc.sbuf_tensor("dsp", [128, 4, NP], F32) as dsp,
            nc.sbuf_tensor("xntm4", [128, 4, D], F32) as xntm4,
            nc.sbuf_tensor("acc", [128, 16, D], F32) as acc,
        ):
            for b in range(2):
                def _(v):
                    kb.w(v, "sw")
                    kb.inc(v.memset(acc[:, :, :], 0.0), "v")
                block.vector(_)
                evac_v = []
                for tsup in range(4):
                    def _(sync, b=b, tsup=tsup):
                        kb.w(sync, "t")
                        r0 = b * TB + tsup * 512
                        for q in range(4):
                            kb.inc(sync.dma_start(out=dsp[:, q, :],
                                                  in_=disp_d[r0 + q * 128:r0 + (q + 1) * 128, :]),
                                   "ld", 16)
                            kb.inc(sync.dma_start(out=xntm4[:, q, :],
                                                  in_=xn_tm_d[r0 + q * 128:r0 + (q + 1) * 128, :]),
                                   "ld", 16)
                    block.sync(_)
                    for gidx in range(32):
                        npc, dh = gidx // 2, gidx % 2
                        bank = pb[gidx % 4]
                        def _(t, npc=npc, dh=dh, bank=bank, gi=len(evac_v)):
                            kb.w(t, "ld")
                            if gi >= 4:
                                kb.wat(t, "v", evac_v[gi - 4])
                            else:
                                kb.w(t, "v")
                            for q in range(4):
                                ins = t.matmul(bank[:, :],
                                               dsp[:, q, npc * 128:(npc + 1) * 128],
                                               xntm4[:, q, dh * 512:(dh + 1) * 512],
                                               start=(q == 0), stop=(q == 3))
                            kb.inc(ins, "t")
                        block.tensor(_)
                        def _(v, npc=npc, dh=dh, bank=bank):
                            kb.w(v, "t")
                            sl = acc[:, npc, dh * 512:(dh + 1) * 512]
                            kb.inc(v.tensor_add(sl, sl, bank[:, :]), "v")
                        block.vector(_)
                        evac_v.append(cnt["v"])
                def _(g, b=b):
                    kb.w(g, "v")
                    for npc in range(16):
                        for q in range(4):
                            n = npc * 4 + q
                            kb.inc(g.dma_start(
                                out=slots_src[n, b, :, :],
                                in_=acc[q * 32:(q + 1) * 32, npc, :]), "sw", 16)
                block.gpsimd(_)
            def _(g):
                kb.w(g, "sw")
                kb.inc(g.collective_compute(
                    "ReduceScatter", ALU.add, replica_groups=[list(range(NC))],
                    ins=[slots_src.opt()], outs=[slots_dst.opt()]), "cc")
            block.gpsimd(_)

        barrier()

        with (
            nc.sbuf_tensor("w1_sb", [128, DC, DM], F32) as w1_sb,
            nc.sbuf_tensor("y2a", [128, DC, 256], F32) as y2a,
            nc.sbuf_tensor("y2b", [128, DC, 256], F32) as y2b,
            nc.sbuf_tensor("hh", [128, 2, 256], F32) as hh,
        ):
            def _(sync):
                kb.inc(sync.dma_start(out=w1_sb[:, :, :],
                                      in_=w1.rearrange("(c p) m -> p c m", p=128)), "ld", 16)
            block.sync(_)
            t_done = []
            s_done = []
            g_done = []
            for jj in range(16):
                buf = y2a if jj % 2 == 0 else y2b
                c0 = jj * 256
                def _(sync, buf=buf, c0=c0, jj=jj):
                    if jj >= 2:
                        kb.wat(sync, "t", t_done[(jj - 1) * 32 - 1])
                    for k in range(DC):
                        kb.inc(sync.dma_start(out=buf[:, k, :],
                                              in_=y2_fm_d[k * 128:(k + 1) * 128, c0:c0 + 256]),
                               "ld", 16)
                block.sync(_)
                for mch in range(32):
                    bank = pb[mch % 2]
                    gi = len(t_done)
                    def _(t, buf=buf, mch=mch, bank=bank, gi=gi):
                        kb.w(t, "ld")
                        if gi >= 2:
                            kb.wat(t, "s", s_done[gi - 2])
                        for k in range(DC):
                            ins = t.matmul(bank[:, 0:256],
                                           w1_sb[:, k, mch * 128:(mch + 1) * 128],
                                           buf[:, k, :], start=(k == 0), stop=(k == DC - 1))
                        kb.inc(ins, "t")
                    block.tensor(_)
                    t_done.append(cnt["t"])
                    def _(s, mch=mch, bank=bank, gi=gi):
                        kb.wat(s, "t", t_done[gi])
                        if gi >= 2:
                            kb.wat(s, "sw", g_done[gi - 2])
                        kb.inc(s.activation(hh[:, mch % 2, :], bank[:, 0:256],
                                            AF.Gelu_apprx_tanh,
                                            bias=b1_sb[:, mch:mch + 1], scale=1.0), "s")
                    block.scalar(_)
                    s_done.append(cnt["s"])
                    def _(s2, mch=mch, c0=c0, gi=gi):
                        kb.wat(s2, "s", s_done[gi])
                        kb.inc(s2.dma_start(out=h_fm_d[mch * 128:(mch + 1) * 128, c0:c0 + 256],
                                            in_=hh[:, mch % 2, :]), "sw", 16)
                    block.gpsimd(_)
                    g_done.append(cnt["sw"])

        barrier()

        with (
            nc.sbuf_tensor("w2_sb", [128, 32, D], F32) as w2_sb,
            nc.sbuf_tensor("ha", [128, 32, 256], F32) as ha,
            nc.sbuf_tensor("x1a", [128, DC, 256], F32) as x1a,
            nc.sbuf_tensor("ssb", [128, 2, 256], F32) as ssb,
        ):
            def _(sync):
                kb.inc(sync.dma_start(out=w2_sb[:, :, :],
                                      in_=w2.rearrange("(c p) m -> p c m", p=128)), "ld", 16)
            block.sync(_)
            v_done = []
            for jj in range(16):
                c0 = jj * 256
                def _(sync, c0=c0, jj=jj):
                    kb.w(sync, "t", "v")
                    for k in range(32):
                        kb.inc(sync.dma_start(out=ha[:, k, :],
                                              in_=h_fm_d[k * 128:(k + 1) * 128, c0:c0 + 256]),
                               "ld", 16)
                    for k in range(DC):
                        kb.inc(sync.dma_start(out=x1a[:, k, :],
                                              in_=x1_fm_d[k * 128:(k + 1) * 128, c0:c0 + 256]),
                               "ld", 16)
                block.sync(_)
                for dch in range(DC):
                    bank = pb[dch % 2]
                    gi = len(v_done)
                    def _(t, dch=dch, bank=bank, gi=gi):
                        kb.w(t, "ld")
                        if gi >= 2:
                            kb.wat(t, "v", v_done[gi - 2])
                        for k in range(32):
                            ins = t.matmul(bank[:, 0:256],
                                           w2_sb[:, k, dch * 128:(dch + 1) * 128],
                                           ha[:, k, :], start=(k == 0), stop=(k == 31))
                        kb.inc(ins, "t")
                    block.tensor(_)
                    def _(v, dch=dch, bank=bank):
                        kb.w(v, "t", "sw")
                        kb.inc(v.scalar_tensor_tensor(ssb[:, dch % 2, :], bank[:, 0:256],
                                                      b2_sb[:, dch:dch + 1], x1a[:, dch, :],
                                                      op0=ALU.add, op1=ALU.add), "v")
                    block.vector(_)
                    v_done.append(cnt["v"])
                    def _(g, dch=dch, c0=c0):
                        kb.w(g, "v")
                        kb.inc(g.dma_start(out=s_fm_d[dch * 128:(dch + 1) * 128, c0:c0 + 256],
                                           in_=ssb[:, dch % 2, :]), "sw", 16)
                    block.gpsimd(_)

        barrier()

        with (
            nc.sbuf_tensor("wexp_a", [128, DC, D], F32) as wexp_a,
            nc.sbuf_tensor("wexp_b", [128, DC, D], F32) as wexp_b,
            nc.sbuf_tensor("sl_sb", [64, D], F32) as sl_sb,
            nc.sbuf_tensor("slT", [128, DC, 64], F32) as slT,
            nc.sbuf_tensor("bx_sb", [1, NEXP * D], F32) as bx_sb,
            nc.sbuf_tensor("yl", [64, D], F32) as yl,
        ):
            def _(sync):
                kb.w(sync, "cc")
                kb.inc(sync.dma_start(out=bx_sb[:, :], in_=bexp), "ld", 16)
                kb.inc(sync.dma_start(out=wexp_a[:, :, :],
                                      in_=wexp[0].rearrange("(c p) m -> p c m", p=128)), "ld", 16)
            block.sync(_)
            for n in range(NEXP):
                wb = wexp_a if n % 2 == 0 else wexp_b
                def _(sync, n=n):
                    kb.w(sync, "t")
                    if n + 1 < NEXP:
                        nxt = wexp_b if n % 2 == 0 else wexp_a
                        kb.inc(sync.dma_start(out=nxt[:, :, :],
                                              in_=wexp[n + 1].rearrange("(c p) m -> p c m", p=128)),
                               "ld", 16)
                    kb.inc(sync.dma_start(
                        out=sl_sb[:, :],
                        in_=slots_dst[n].rearrange("b p d -> (b p) d")), "ld", 16)
                block.sync(_)
                for half in range(2):
                    def _(t, half=half):
                        kb.w(t, "ld", "v")
                        for q in range(4):
                            k = half * 4 + q
                            kb.inc(t.transpose(pb[q][0:128, 0:64],
                                               sl_sb[:, k * 128:(k + 1) * 128],
                                               eye_sb[0:64, 0:64]), "t")
                    block.tensor(_)
                    def _(v, half=half):
                        base = cnt["t"] - 4
                        for q in range(4):
                            k = half * 4 + q
                            kb.wat(v, "t", base + q + 1)
                            kb.inc(v.tensor_copy(slT[:, k, :], pb[q][0:128, 0:64]), "v")
                    block.vector(_)
                def _(t, wb=wb, n=n):
                    kb.w(t, "v")
                    for eh in range(2):
                        for k in range(DC):
                            t.matmul(pb[4 + eh][0:64, :], slT[:, k, :],
                                     wb[:, k, eh * 512:(eh + 1) * 512],
                                     start=(k == 0), stop=False)
                        ins = t.matmul(pb[4 + eh][0:64, :], ones[0:1, 0:64],
                                       bx_sb[0:1, n * D + eh * 512:n * D + (eh + 1) * 512],
                                       start=False, stop=True)
                    kb.inc(ins, "t")
                block.tensor(_)
                def _(v):
                    kb.w(v, "t", "sw")
                    v.tensor_copy(yl[:, 0:512], pb[4][0:64, :])
                    kb.inc(v.tensor_copy(yl[:, 512:1024], pb[5][0:64, :]), "v")
                block.vector(_)
                def _(g, n=n):
                    kb.w(g, "v")
                    kb.inc(g.dma_start(out=y_src[n], in_=yl[:, :]), "sw", 16)
                block.gpsimd(_)
            def _(g):
                kb.w(g, "sw")
                kb.inc(g.collective_compute(
                    "AllGather", ALU.bypass, replica_groups=[list(range(NC))],
                    ins=[y_src.opt()], outs=[y_dst.opt()]), "cc")
            block.gpsimd(_)

        barrier()

        with (
            nc.sbuf_tensor("y_sb", [128, 16, D], F32) as y_sb,
            nc.sbuf_tensor("cmb", [128, NP], F32) as cmb,
            nc.sbuf_tensor("cmT", [128, 16, TT], F32) as cmT,
            nc.sbuf_tensor("xf", [128, DC, TT], F32) as xf,
            nc.sbuf_tensor("sf", [128, DC, TT], F32) as sf,
            nc.sbuf_tensor("gg", [128, TT], F32) as gg,
            nc.sbuf_tensor("oo", [128, TT], F32) as oo,
        ):
            for j in range(NT):
                b = j // 4
                c0 = j * TT
                if j % 4 == 0:
                    def _(sync, b=b):
                        kb.w(sync, "cc", "t")
                        for npc in range(16):
                            for q in range(4):
                                kb.inc(sync.dma_start(
                                    out=y_sb[q * 32:(q + 1) * 32, npc, :],
                                    in_=y_dst[npc * 4 + q, b * 32:(b + 1) * 32, :]), "ld", 16)
                    block.sync(_)
                def _(sync, c0=c0):
                    kb.w(sync, "v")
                    for k in range(DC):
                        kb.inc(sync.dma_start(out=xf[:, k, :],
                                              in_=x_fm[k * 128:(k + 1) * 128, c0:c0 + TT]), "ld", 16)
                        kb.inc(sync.dma_start(out=sf[:, k, :],
                                              in_=s_fm_d[k * 128:(k + 1) * 128, c0:c0 + TT]), "ld", 16)
                block.sync(_)
                for sub in range(4):
                    def _(sync, c0=c0, sub=sub):
                        kb.w(sync, "t")
                        kb.inc(sync.dma_start(out=cmb[:, :],
                                              in_=comb_d[c0 + sub * 128:c0 + (sub + 1) * 128, :]),
                               "ld", 16)
                    block.sync(_)
                    for grp in range(4):
                        def _(t, grp=grp, sub=sub):
                            kb.w(t, "ld", "v", "s")
                            for q in range(4):
                                npc = grp * 4 + q
                                kb.inc(t.transpose(pb[q][:, 0:128],
                                                   cmb[:, npc * 128:(npc + 1) * 128],
                                                   eye_sb[:, :]), "t")
                        block.tensor(_)
                        def _(v, grp=grp, sub=sub):
                            base = cnt["t"] - 4
                            kb.wat(v, "t", base + 1)
                            kb.inc(v.tensor_copy(cmT[:, grp * 4 + 0, sub * 128:(sub + 1) * 128],
                                                 pb[0][:, 0:128]), "v")
                            kb.wat(v, "t", base + 3)
                            kb.inc(v.tensor_copy(cmT[:, grp * 4 + 2, sub * 128:(sub + 1) * 128],
                                                 pb[2][:, 0:128]), "v")
                        block.vector(_)
                        def _(s, grp=grp, sub=sub):
                            base = cnt["t"] - 4
                            kb.wat(s, "t", base + 2)
                            kb.inc(s.activation(cmT[:, grp * 4 + 1, sub * 128:(sub + 1) * 128],
                                                pb[1][:, 0:128], AF.Identity,
                                                bias=0.0, scale=1.0), "s")
                            kb.wat(s, "t", base + 4)
                            kb.inc(s.activation(cmT[:, grp * 4 + 3, sub * 128:(sub + 1) * 128],
                                                pb[3][:, 0:128], AF.Identity,
                                                bias=0.0, scale=1.0), "s")
                        block.scalar(_)
                for dch in range(DC):
                    bank = pb[4 + (dch % 2)]
                    def _(t, dch=dch, bank=bank):
                        kb.w(t, "v", "s")
                        for npc in range(16):
                            ins = t.matmul(bank[:, :],
                                           y_sb[:, npc, dch * 128:(dch + 1) * 128],
                                           cmT[:, npc, :], start=(npc == 0), stop=(npc == 15))
                        kb.inc(ins, "t")
                    block.tensor(_)
                    def _(s, dch=dch, bank=bank):
                        kb.w(s, "t")
                        kb.inc(s.activation(gg[:, :], bank[:, :], AF.Silu,
                                            bias=0.0, scale=1.0), "s")
                    block.scalar(_)
                    def _(v, dch=dch):
                        kb.w(v, "s", "sw")
                        v.tensor_tensor(oo[:, :], sf[:, dch, :], gg[:, :], op=ALU.mult)
                        kb.inc(v.tensor_add(oo[:, :], oo[:, :], xf[:, dch, :]), "v")
                    block.vector(_)
                    def _(g, dch=dch, c0=c0):
                        kb.w(g, "v")
                        kb.inc(g.dma_start(out=out_fm[dch * 128:(dch + 1) * 128, c0:c0 + TT],
                                           in_=oo[:, :]), "sw", 16)
                    block.gpsimd(_)

        if _DEBUG_TAPS:
            taps = [
                ("y1_dbg", y1_dbg, (D, M)), ("trig_dbg", trig_dbg, (4 * NH, M)),
                ("heads_dbg", heads_dbg, (D, M)),
                ("xn_fm_d", xn_fm_d, (D, M)), ("xn_tm_d", xn_tm_d, (M, D)),
                ("ckv_d", ckv_d, (D, M)), ("skv_d", skv_d, (D, M)),
                ("carry_d", carry_d, (1, 4160)), ("stats_d", stats_d, (2, M)),
                ("x1_fm_d", x1_fm_d, (D, M)), ("y2_fm_d", y2_fm_d, (D, M)),
                ("s_fm_d", s_fm_d, (D, M)),
                ("disp_d", disp_d, (M, NP)), ("comb_d", comb_d, (M, NP)),
                ("slots_src", slots_src, (64, 2, 32, D)),
                ("slots_dst", slots_dst, (NEXP, 2, 32, D)),
                ("y_dst", y_dst, (64, 64, D)),
            ]
            def _(g):
                kb.w(g, "sw")
                for nm, src, shp in taps:
                    dst = nc.dram_tensor("tap_" + nm, list(shp), F32,
                                         kind="ExternalOutput").ap()
                    kb.inc(g.dma_start(out=dst, in_=src), "sw", 16)
                kb.w(g, "sw")
            block.gpsimd(_)

        def _(g):
            kb.w(g, "sw")
        block.gpsimd(_)

    es.close()
    return nc


def kernel(**inputs):
    x = np.asarray(inputs["L0_attn_x"], dtype=np.float32)
    WQ = np.asarray(inputs["L0_attn_W_Q"], dtype=np.float32)
    WK = np.asarray(inputs["L0_attn_W_K"], dtype=np.float32)
    WV = np.asarray(inputs["L0_attn_W_V"], dtype=np.float32)
    WO = np.asarray(inputs["L0_attn_W_O"], dtype=np.float32)
    ln1w = np.asarray(inputs["L0_attn_ln1_w"], dtype=np.float32)
    ln1b = np.asarray(inputs["L0_attn_ln1_b"], dtype=np.float32)
    ln2w = np.asarray(inputs["L0_attn_ln2_w"], dtype=np.float32)
    ln2b = np.asarray(inputs["L0_attn_ln2_b"], dtype=np.float32)
    W1 = np.asarray(inputs["L0_attn_W1"], dtype=np.float32)
    b1 = np.asarray(inputs["L0_attn_b1"], dtype=np.float32)
    W2 = np.asarray(inputs["L0_attn_W2"], dtype=np.float32)
    b2 = np.asarray(inputs["L0_attn_b2"], dtype=np.float32)
    phi = np.asarray(inputs["L0_ffn_phi"], dtype=np.float32)
    We = np.asarray(inputs["L0_ffn_W"], dtype=np.float32)
    be = np.asarray(inputs["L0_ffn_b"], dtype=np.float32)

    wq_m = np.ascontiguousarray(WQ[:, :, 0].T)
    wk_m = np.ascontiguousarray(WK[:, :, 0].T)
    wv_m = np.ascontiguousarray(WV.transpose(1, 0, 2).reshape(1024, 1024))
    wo_m = np.ascontiguousarray(WO.transpose(2, 1, 0).reshape(1024, 1024))
    phi_m = np.ascontiguousarray(phi.reshape(1024, 2048))
    eye = np.eye(128, dtype=np.float32)
    esel = np.zeros((NH, 1024), dtype=np.float32)
    for mch in range(8):
        for p in range(128):
            esel[2 * mch + p // 64, mch * 128 + p] = 1.0

    def pcol(vec, kk):
        return np.ascontiguousarray(vec.reshape(kk, 128).T)

    nc = build_program()

    in_maps = []
    for c in range(NC):
        sl = x[:, c * TB:(c + 1) * TB, :]
        x_tm_c = np.ascontiguousarray(sl.reshape(M, D))
        x_fm_c = np.ascontiguousarray(x_tm_c.T)
        mask = np.zeros((8, 1), np.float32)
        mask[:c] = 1.0
        in_maps.append(dict(
            x_fm=x_fm_c, x_tm=x_tm_c,
            wq=wq_m, wk=wk_m, wv=wv_m, wo=wo_m,
            w1=W1, b1=pcol(b1, 32), w2=W2, b2=pcol(b2, DC),
            ln1w=pcol(ln1w, DC), ln1b=pcol(ln1b, DC),
            ln2w=pcol(ln2w, DC), ln2b=pcol(ln2b, DC),
            phi=phi_m,
            wexp=np.ascontiguousarray(We[c * NEXP:(c + 1) * NEXP]),
            bexp=np.ascontiguousarray(be[c * NEXP:(c + 1) * NEXP].reshape(1, NEXP * D)),
            maskv=mask, eye=eye, esel=esel,
        ))

    res = run_bass_kernel_spmd(nc, in_maps, core_ids=list(range(NC)))
    out = np.empty((2, 16384, 1024), dtype=np.float32)
    for c in range(NC):
        of = res.results[c]["out_fm"]
        out[:, c * TB:(c + 1) * TB, :] = np.asarray(of).T.reshape(2, TB, D)
    return out
